# version 02 (final)
# baseline (speedup 1.0000x reference)
import numpy as np
import ml_dtypes
from contextlib import ExitStack
import concourse.bass as bass
import concourse.mybir as mybir
from concourse.bass_utils import run_bass_kernel_spmd

F32 = mybir.dt.float32
BF16 = mybir.dt.bfloat16
AF = mybir.ActivationFunctionType
ALU = mybir.AluOpType
AX = mybir.AxisListType
BF = ml_dtypes.bfloat16

NCORES = 8
S = 16384
D = 4096
T = 2048
NT = 16
ENGS = ["sync", "gpsimd", "tensor", "vector", "scalar"]


class Prog:
    def __init__(self):
        self.ops = []
        self.dtot = 0
        self.ctot = 0

    def op(self, eng, meth, *args, dma=False, wait=True, inc=True, **kwargs):
        barrier = (self.dtot, self.ctot) if wait else None
        self.ops.append((eng, meth, args, kwargs, barrier, dma, inc))
        if inc:
            if dma:
                self.dtot += 16
            else:
                self.ctot += 1

    def dma(self, out, in_, q="sync", **kw):
        self.op(q, "dma_start", out, in_, dma=True, **kw)

    def emit(self, block, sd, sc):
        used = [e for e in ENGS if any(o[0] == e for o in self.ops)]
        final_d, final_c = self.dtot, self.ctot

        def make(engname, is_last):
            items = [o for o in self.ops if o[0] == engname]

            def section(eng):
                seen_d, seen_c = 0, 0
                for (_, meth, args, kwargs, barrier, dma, inc) in items:
                    if barrier is not None:
                        dn, cn = barrier
                        if dn > seen_d:
                            eng.wait_ge(sd, dn)
                            seen_d = dn
                        if cn > seen_c:
                            eng.wait_ge(sc, cn)
                            seen_c = cn
                    instr = getattr(eng, meth)(*args, **kwargs)
                    if inc:
                        instr.then_inc(sd if dma else sc, 16 if dma else 1)
                if is_last:
                    if final_d > seen_d:
                        eng.wait_ge(sd, final_d)
                    if final_c > seen_c:
                        eng.wait_ge(sc, final_c)

            return section

        for name in used:
            getattr(block, name)(make(name, name == used[-1]))


def mk_nc():
    return bass.Bass("TRN2", target_bir_lowering=False)


def ln_tile(P, xf, xnf, scr, mt, vt, m2, n):
    P.op("vector", "reduce_sum", mt, xf, axis=AX.X)
    P.op("vector", "tensor_scalar_mul", mt, mt, 1.0 / n)
    P.op("scalar", "activation", scr, xf, AF.Square)
    P.op("vector", "reduce_sum", vt, scr, axis=AX.X)
    P.op("vector", "tensor_scalar_mul", vt, vt, 1.0 / n)
    P.op("vector", "tensor_mul", m2, mt, mt)
    P.op("vector", "tensor_sub", vt, vt, m2)
    P.op("vector", "tensor_scalar_add", vt, vt, 1e-5)
    P.op("scalar", "activation", vt, vt, AF.Ln)
    P.op("vector", "tensor_scalar_mul", vt, vt, -0.5)
    P.op("scalar", "activation", vt, vt, AF.Exp)
    P.op("vector", "tensor_scalar", xnf, xf, mt, vt, ALU.subtract, op1=ALU.mult)


def build_p1():
    nc = mk_nc()
    x_d = nc.dram_tensor("x", [T, D], F32, kind="ExternalInput").ap()
    wq_d = nc.dram_tensor("wqT", [D, 1024], F32, kind="ExternalInput").ap()
    bq_d = nc.dram_tensor("bq", [1, 1024], F32, kind="ExternalInput").ap()
    mz_d = nc.dram_tensor("mz", [8, 128, 129], F32, kind="ExternalInput").ap()
    wo_d = nc.dram_tensor("woT", [D, D], F32, kind="ExternalInput").ap()
    w0_d = nc.dram_tensor("w0T", [D, D], F32, kind="ExternalInput").ap()
    b0_d = nc.dram_tensor("b0", [1, D], F32, kind="ExternalInput").ap()
    ac_d = nc.dram_tensor("acT", [D, 128], F32, kind="ExternalInput").ap()
    mk_d = nc.dram_tensor("msk", [128, 16], F32, kind="ExternalInput").ap()
    idb_d = nc.dram_tensor("idb", [128, 128], BF16, kind="ExternalInput").ap()
    onb_d = nc.dram_tensor("onb", [1, 128], BF16, kind="ExternalInput").ap()
    acc_d = nc.dram_tensor("acc", [T, D], F32, kind="ExternalOutput").ap()
    tm_d = nc.dram_tensor("tmat", [T, 128], F32, kind="ExternalOutput").ap()
    rp_d = nc.dram_tensor("rpart", [128, D], F32, kind="ExternalOutput").ap()

    P = Prog()
    with ExitStack() as es:
        sb = lambda n_, s_, dt=F32: es.enter_context(nc.sbuf_tensor("s_" + n_, s_, dt))
        xt = sb("xt", [128, D])
        xnf = sb("xnf", [128, D])
        scr = sb("scr", [128, D])
        xnb = sb("xnb", [128, D], BF16)
        xnT = sb("xnT", [128, 32, 512], BF16)
        sig = sb("sig", [128, 4, 1024], BF16)
        sigT = sb("sigT", [128, 8, 512], BF16)
        retT = sb("retT", [128, 8, 512], BF16)
        retb = sb("retb", [128, 128], BF16)
        rp = sb("rp", [128, D])
        wst = sb("wst", [128, D])
        wob = sb("wob", [128, 512], BF16)
        w0b = sb("w0b", [128, 512], BF16)
        wqf = sb("wqf", [128, 1024])
        wqb = sb("wqb", [128, 1024], BF16)
        acb = sb("acb", [128, 32, 128], BF16)
        mzf = sb("mzf", [128, 8 * 129])
        mzb = sb("mzb", [128, 8, 129], BF16)
        bqb = sb("bqb", [1, 1024], BF16)
        b0b = sb("b0b", [1, D], BF16)
        idb = sb("idb", [128, 128], BF16)
        onb = sb("onb", [1, 128], BF16)
        msk = sb("msk", [128, 16])
        mt = sb("mt", [128, 1]); vt = sb("vt", [128, 1]); m2 = sb("m2", [128, 1])
        dnr = sb("dnr", [128, 1])
        ost = sb("ost", [128, 512])
        pb = es.enter_context(nc.psum_tensor("pb", [128, 4, 512], F32))
        ptb = es.enter_context(nc.psum_tensor("ptb", [128, 128], BF16))
        sd = es.enter_context(nc.semaphore("sd"))
        sc = es.enter_context(nc.semaphore("sc"))

        P.dma(idb[:, :], idb_d)
        P.dma(onb[:, :], onb_d)
        P.dma(msk[:, :], mk_d)
        P.dma(bqb[0:1, :].bitcast(F32) if False else wqf[0:1, 0:1024], bq_d)
        P.op("vector", "tensor_copy", bqb[0:1, :], wqf[0:1, 0:1024])
        P.dma(wst[0:1, :], b0_d)
        P.op("vector", "tensor_copy", b0b[0:1, :], wst[0:1, :])
        for h in range(8):
            P.dma(mzf[:, h * 129:(h + 1) * 129], mz_d[h], wait=(h == 0))
        P.op("vector", "tensor_copy",
             mzb[:, :, :].rearrange("p a b -> p (a b)"), mzf[:, :])
        for kc in range(32):
            P.dma(wst[:, kc * 128:(kc + 1) * 128],
                  ac_d[kc * 128:(kc + 1) * 128, :], wait=(kc == 0))
        P.op("vector", "tensor_copy",
             acb[:, :, :].rearrange("p a b -> p (a b)"), wst[:, :])
        P.op("vector", "memset", rp[:, :], 0.0)

        for st in range(4):
            for t in range(4):
                g = st * 4 + t
                P.dma(xt[:, :], x_d[g * 128:(g + 1) * 128, :])
                ln_tile(P, xt[:, :], xnf[:, :], scr[:, :], mt[:, :], vt[:, :], m2[:, :], D)
                P.op("vector", "scalar_tensor_tensor", rp[:, :], xnf[:, :],
                     msk[:, g:g + 1], rp[:, :], ALU.mult, ALU.add)
                P.op("vector", "tensor_copy", xnb[:, :], xnf[:, :])
                for kc in range(32):
                    P.op("tensor", "transpose", ptb[:, :],
                         xnb[:, kc * 128:(kc + 1) * 128], idb[:, :])
                    P.op("scalar", "copy", xnT[:, kc, t * 128:(t + 1) * 128], ptb[:, :])
            for half in range(2):
                for kc in range(32):
                    P.dma(wqf[:, :], wq_d[kc * 128:(kc + 1) * 128, :])
                    P.op("vector", "tensor_copy", wqb[:, :], wqf[:, :])
                    for tt in range(2):
                        t = half * 2 + tt
                        for nch in range(2):
                            P.op("tensor", "matmul", pb[:, tt * 2 + nch, :],
                                 xnT[:, kc, t * 128:(t + 1) * 128],
                                 wqb[:, nch * 512:(nch + 1) * 512],
                                 start=(kc == 0), stop=False,
                                 wait=(tt == 0 and nch == 0), inc=(tt == 1 and nch == 1))
                for tt in range(2):
                    t = half * 2 + tt
                    for nch in range(2):
                        P.op("tensor", "matmul", pb[:, tt * 2 + nch, :],
                             onb[0:1, :], bqb[0:1, nch * 512:(nch + 1) * 512],
                             start=False, stop=True)
                        P.op("vector", "tensor_scalar_min", ost[:, :], pb[:, tt * 2 + nch, :], 0.0)
                        P.op("scalar", "activation", ost[:, :], ost[:, :], AF.Exp)
                        P.op("vector", "tensor_scalar_max", scr[:, 0:512], pb[:, tt * 2 + nch, :], 0.0)
                        P.op("vector", "tensor_add",
                             sig[:, t, nch * 512:(nch + 1) * 512], ost[:, :], scr[:, 0:512])
            for t in range(4):
                for h in range(8):
                    P.op("tensor", "transpose", ptb[:, :],
                         sig[:, t, h * 128:(h + 1) * 128], idb[:, :])
                    P.op("scalar", "copy", sigT[:, h, t * 128:(t + 1) * 128], ptb[:, :])
            for t in range(4):
                for h in range(8):
                    P.op("tensor", "matmul", pb[:, 0, 0:129],
                         sigT[:, h, t * 128:(t + 1) * 128], mzb[:, h, :],
                         start=True, stop=True)
                    P.op("vector", "tensor_scalar_add", mt[:, :], pb[:, 0, 128:129], 1e-6)
                    P.op("vector", "reciprocal", dnr[:, :], mt[:, :])
                    P.op("vector", "tensor_scalar_mul", retb[:, :], pb[:, 0, 0:128], dnr[:, :])
                    P.op("tensor", "transpose", ptb[:, :], retb[:, :], idb[:, :])
                    P.op("scalar", "copy", retT[:, h, t * 128:(t + 1) * 128], ptb[:, :])
            for nch in range(8):
                for kc in range(32):
                    P.dma(wst[:, 0:512], wo_d[kc * 128:(kc + 1) * 128, nch * 512:(nch + 1) * 512])
                    P.op("vector", "tensor_copy", wob[:, :], wst[:, 0:512])
                    P.dma(wst[:, 512:1024], w0_d[kc * 128:(kc + 1) * 128, nch * 512:(nch + 1) * 512])
                    P.op("vector", "tensor_copy", w0b[:, :], wst[:, 512:1024])
                    for t in range(4):
                        P.op("tensor", "matmul", pb[:, t, :],
                             retT[:, kc // 4, t * 128:(t + 1) * 128], wob[:, :],
                             start=(kc == 0), stop=False,
                             wait=(t == 0), inc=False)
                        P.op("tensor", "matmul", pb[:, t, :],
                             xnT[:, kc, t * 128:(t + 1) * 128], w0b[:, :],
                             start=False, stop=False, wait=False, inc=(t == 3))
                for t in range(4):
                    g = st * 4 + t
                    P.op("tensor", "matmul", pb[:, t, :], onb[0:1, :],
                         b0b[0:1, nch * 512:(nch + 1) * 512], start=False, stop=True)
                    P.dma(xt[:, 0:512], x_d[g * 128:(g + 1) * 128, nch * 512:(nch + 1) * 512])
                    P.op("vector", "tensor_add", ost[:, :], pb[:, t, :], xt[:, 0:512])
                    P.dma(acc_d[g * 128:(g + 1) * 128, nch * 512:(nch + 1) * 512],
                          ost[:, :], q="gpsimd")
            for t in range(4):
                g = st * 4 + t
                for kc in range(32):
                    P.op("tensor", "matmul", pb[:, 0, 0:128],
                         xnT[:, kc, t * 128:(t + 1) * 128], acb[:, kc, :],
                         start=(kc == 0), stop=(kc == 31),
                         wait=(kc == 0), inc=(kc == 31))
                P.op("vector", "tensor_copy", ost[:, 0:128], pb[:, 0, 0:128])
                P.dma(tm_d[g * 128:(g + 1) * 128, :], ost[:, 0:128], q="gpsimd")
        P.dma(rp_d, rp[:, :], q="gpsimd")
        with nc.Block() as block:
            P.emit(block, sd, sc)
    return nc


def build_p2():
    nc = mk_nc()
    acc_d = nc.dram_tensor("acc", [T, D], F32, kind="ExternalInput").ap()
    tm_d = nc.dram_tensor("tmat", [T, 128], F32, kind="ExternalInput").ap()
    rps_d = nc.dram_tensor("rps", [8, 128, D], F32, kind="ExternalInput").ap()
    bct_d = nc.dram_tensor("bcT", [128, D], F32, kind="ExternalInput").ap()
    rw0_d = nc.dram_tensor("rw0T", [D, 8], F32, kind="ExternalInput").ap()
    rb0_d = nc.dram_tensor("rb0", [1, 8], F32, kind="ExternalInput").ap()
    rw1_d = nc.dram_tensor("rw1T", [D, 64], F32, kind="ExternalInput").ap()
    rb1_d = nc.dram_tensor("rb1", [1, 64], F32, kind="ExternalInput").ap()
    eb_d = nc.dram_tensor("eb", [64, D], F32, kind="ExternalInput").ap()
    idb_d = nc.dram_tensor("idb", [128, 128], BF16, kind="ExternalInput").ap()
    onb_d = nc.dram_tensor("onb", [1, 128], BF16, kind="ExternalInput").ap()
    onf_d = nc.dram_tensor("onf", [1, 128], F32, kind="ExternalInput").ap()
    onc_d = nc.dram_tensor("onc", [128, 1], F32, kind="ExternalInput").ap()
    f_d = nc.dram_tensor("f", [T, D], F32, kind="ExternalOutput").ap()
    xn1_d = nc.dram_tensor("xn1b", [T, D], BF16, kind="ExternalOutput").ap()

    P = Prog()
    with ExitStack() as es:
        sb = lambda n_, s_, dt=F32: es.enter_context(nc.sbuf_tensor("s_" + n_, s_, dt))
        rs = sb("rs", [128, D]); tmp = sb("tmp", [128, D])
        qsb = sb("qsb", [128, 32])
        rw0 = sb("rw0", [128, 32, 8])
        rb0 = sb("rb0", [1, 8]); rb1b = sb("rb1b", [1, 64], BF16)
        el = sb("el", [1, 8]); ee = sb("ee", [1, 8]); route2 = sb("route2", [1, 8])
        sm1 = sb("sm1", [1, 1]); rcp = sb("rcp", [1, 1])
        rrep = sb("rrep", [1, 128])
        rrsb = sb("rrsb", [128, 128])
        onf = sb("onf", [1, 128]); onb = sb("onb", [1, 128], BF16)
        onc = sb("onc", [128, 1])
        idb = sb("idb", [128, 128], BF16)
        bctb = sb("bctb", [128, D], BF16)
        rw1b = sb("rw1b", [128, 32, 64], BF16)
        ebb = sb("ebb", [64, D], BF16)
        tt = sb("tt", [128, 128]); ttb = sb("ttb", [128, 128], BF16)
        tTb = sb("tTb", [128, 128], BF16)
        acst = sb("acst", [128, D])
        x1f = sb("x1f", [128, D])
        xn1f = sb("xn1f", [128, D])
        scr = sb("scr", [128, D])
        xn1b = sb("xn1b", [128, D], BF16)
        xnT = sb("xnT", [128, 32, 128], BF16)
        sc_ = sb("sc_", [128, 64]); wk = sb("wk", [128, 64]); kp = sb("kp", [128, 64])
        mk_ = sb("mk_", [128, 64]); tk = sb("tk", [128, 64])
        kpb = sb("kpb", [128, 128], BF16)
        ktb = sb("ktb", [64, 128], BF16)
        mt = sb("mt", [128, 1]); vt = sb("vt", [128, 1]); m2 = sb("m2", [128, 1])
        ws = sb("ws", [128, 1])
        fst = sb("fst", [128, 512])
        pq32 = es.enter_context(nc.psum_tensor("pq32", [128, 32], F32))
        plog = es.enter_context(nc.psum_tensor("plog", [1, 8], F32))
        prr = es.enter_context(nc.psum_tensor("prr", [128, 128], F32))
        po = es.enter_context(nc.psum_tensor("po", [128, 512], F32))
        plg = es.enter_context(nc.psum_tensor("plg", [128, 64], F32))
        ptb = es.enter_context(nc.psum_tensor("ptb", [128, 128], BF16))
        sd = es.enter_context(nc.semaphore("sd"))
        sc2 = es.enter_context(nc.semaphore("sc2"))

        P.dma(idb[:, :], idb_d); P.dma(onb[:, :], onb_d)
        P.dma(onf[:, :], onf_d); P.dma(onc[:, :], onc_d)
        P.dma(rb0[:, :], rb0_d)
        P.dma(tmp[0:1, 0:64], rb1_d)
        P.op("vector", "tensor_copy", rb1b[0:1, :], tmp[0:1, 0:64])
        P.dma(tmp[:, :], bct_d)
        P.op("vector", "tensor_copy", bctb[:, :], tmp[:, :])
        P.dma(rw0[:, :, :], rw0_d.rearrange("(a p) e -> p a e", p=128))
        P.dma(tmp[:, 0:2048].rearrange("p (a e) -> p a e", a=32), rw1_d.rearrange("(a p) e -> p a e", p=128))
        P.op("vector", "tensor_copy",
             rw1b[:, :, :].rearrange("p a b -> p (a b)"), tmp[:, 0:2048])
        P.dma(tmp[0:64, :], eb_d)
        P.op("vector", "tensor_copy", ebb[:, :], tmp[0:64, :])
        P.dma(rs[:, :], rps_d[0])
        for i in range(1, 8):
            P.dma(tmp[:, :], rps_d[i])
            P.op("vector", "tensor_add", rs[:, :], rs[:, :], tmp[:, :])
        for kc in range(32):
            P.op("tensor", "matmul", pq32[:, kc:kc + 1],
                 rs[:, kc * 128:(kc + 1) * 128], onc[:, :], start=True, stop=True)
        P.op("vector", "tensor_scalar_mul", qsb[:, :], pq32[:, :], 1.0 / 15772.0)
        for kc in range(32):
            P.op("tensor", "matmul", plog[:, :], qsb[:, kc:kc + 1], rw0[:, kc, :],
                 start=(kc == 0), stop=False, wait=(kc == 0), inc=(kc == 31))
        P.op("tensor", "matmul", plog[:, :], onf[0:1, 0:1], rb0[0:1, :],
             start=False, stop=True)
        P.op("vector", "reduce_max", sm1[0:1, :], plog[:, :], axis=AX.X)
        P.op("vector", "tensor_scalar", el[:, :], plog[:, :], sm1[0:1, 0:1], None, ALU.subtract)
        P.op("scalar", "activation", ee[:, :], el[:, :], AF.Exp)
        P.op("vector", "reduce_sum", sm1[0:1, :], ee[:, :], axis=AX.X)
        P.op("vector", "reciprocal", rcp[:, :], sm1[0:1, :])
        P.op("vector", "tensor_scalar", route2[:, :], ee[:, :], rcp[0:1, 0:1], 2.0,
             ALU.mult, op1=ALU.mult)
        for e in range(8):
            P.op("vector", "tensor_scalar_mul", rrep[0:1, e * 16:(e + 1) * 16],
                 onf[0:1, 0:16], route2[0:1, e:e + 1])
        P.op("tensor", "matmul", prr[:, :], onf[0:1, :], rrep[0:1, :], start=True, stop=True)
        P.op("vector", "tensor_copy", rrsb[:, :], prr[:, :])

        for g in range(NT):
            r0 = g * 128
            P.dma(tt[:, :], tm_d[r0:r0 + 128, :])
            P.op("vector", "tensor_mul", ttb[:, :], tt[:, :], rrsb[:, :])
            P.op("tensor", "transpose", ptb[:, :], ttb[:, :], idb[:, :])
            P.op("scalar", "copy", tTb[:, :], ptb[:, :])
            P.dma(acst[:, :], acc_d[r0:r0 + 128, :])
            for nch in range(8):
                P.op("tensor", "matmul", po[:, :], tTb[:, :],
                     bctb[:, nch * 512:(nch + 1) * 512], start=True, stop=True)
                P.op("vector", "tensor_add", x1f[:, nch * 512:(nch + 1) * 512],
                     po[:, :], acst[:, nch * 512:(nch + 1) * 512])
            ln_tile(P, x1f[:, :], xn1f[:, :], scr[:, :], mt[:, :], vt[:, :], m2[:, :], D)
            P.op("vector", "tensor_copy", xn1b[:, :], xn1f[:, :])
            P.dma(xn1_d[r0:r0 + 128, :], xn1b[:, :], q="gpsimd")
            for kc in range(32):
                P.op("tensor", "transpose", ptb[:, :],
                     xn1b[:, kc * 128:(kc + 1) * 128], idb[:, :])
                P.op("scalar", "copy", xnT[:, kc, :], ptb[:, :])
            for kc in range(32):
                P.op("tensor", "matmul", plg[:, :], xnT[:, kc, :], rw1b[:, kc, :],
                     start=(kc == 0), stop=False, wait=(kc == 0), inc=(kc == 31))
            P.op("tensor", "matmul", plg[:, :], onb[0:1, :], rb1b[0:1, :],
                 start=False, stop=True)
            P.op("vector", "reduce_max", mt[:, :], plg[:, :], axis=AX.X)
            P.op("vector", "tensor_scalar", sc_[:, :], plg[:, :], mt[:, :], None, ALU.subtract)
            P.op("scalar", "activation", sc_[:, :], sc_[:, :], AF.Exp)
            P.op("vector", "reduce_sum", vt[:, :], sc_[:, :], axis=AX.X)
            P.op("vector", "reciprocal", vt[:, :], vt[:, :])
            P.op("vector", "tensor_scalar_mul", sc_[:, :], sc_[:, :], vt[:, :])
            P.op("vector", "tensor_copy", wk[:, :], sc_[:, :])
            P.op("vector", "memset", kp[:, :], 0.0)
            for it in range(8):
                P.op("vector", "reduce_max", mt[:, :], wk[:, :], axis=AX.X)
                P.op("vector", "tensor_scalar", mk_[:, :], wk[:, :], mt[:, :], None, ALU.is_ge)
                P.op("vector", "tensor_mul", tk[:, :], wk[:, :], mk_[:, :])
                P.op("vector", "tensor_add", kp[:, :], kp[:, :], tk[:, :])
                P.op("vector", "tensor_sub", wk[:, :], wk[:, :], tk[:, :])
            P.op("vector", "reduce_sum", ws[:, :], kp[:, :], axis=AX.X)
            P.op("vector", "tensor_scalar_add", ws[:, :], ws[:, :], 1.0)
            P.op("vector", "memset", kpb[:, :], 0.0)
            P.op("vector", "tensor_copy", kpb[:, 0:64], kp[:, :])
            P.op("tensor", "transpose", ptb[:, :], kpb[:, :], idb[:, :])
            P.op("scalar", "copy", ktb[0:64, :], ptb[0:64, :])
            for nch in range(8):
                P.op("tensor", "matmul", po[:, :], ktb[0:64, :],
                     ebb[0:64, nch * 512:(nch + 1) * 512], start=True, stop=True)
                P.op("vector", "scalar_tensor_tensor", fst[:, :],
                     xn1f[:, nch * 512:(nch + 1) * 512], ws[:, :], po[:, :],
                     ALU.mult, ALU.add)
                P.op("vector", "tensor_add", fst[:, :], fst[:, :],
                     x1f[:, nch * 512:(nch + 1) * 512])
                P.dma(f_d[r0:r0 + 128, nch * 512:(nch + 1) * 512], fst[:, :], q="gpsimd")
        with nc.Block() as block:
            P.emit(block, sd, sc2)
    return nc


def build_p3():
    NCH = 19
    nc = mk_nc()
    xn_d = nc.dram_tensor("xn1b", [T, D], BF16, kind="ExternalInput").ap()
    w_d = nc.dram_tensor("wp", [D, NCH * 512], F32, kind="ExternalInput").ap()
    idb_d = nc.dram_tensor("idb", [128, 128], BF16, kind="ExternalInput").ap()
    o_d = nc.dram_tensor("o", [T, NCH * 512], F32, kind="ExternalOutput").ap()
    P = Prog()
    with ExitStack() as es:
        sb = lambda n_, s_, dt=F32: es.enter_context(nc.sbuf_tensor("s_" + n_, s_, dt))
        xnb = sb("xnb", [128, D], BF16)
        xnT = sb("xnT", [128, 32, 2048], BF16)
        idb = sb("idb", [128, 128], BF16)
        wst = sb("wst", [128, 512])
        wsb = sb("wsb", [128, 512], BF16)
        ost = sb("ost", [128, 512])
        ptb = es.enter_context(nc.psum_tensor("ptb", [128, 128], BF16))
        po = es.enter_context(nc.psum_tensor("po", [128, 7, 512], F32))
        sd = es.enter_context(nc.semaphore("sd"))
        sc = es.enter_context(nc.semaphore("sc"))
        P.dma(idb[:, :], idb_d)
        for g in range(NT):
            P.dma(xnb[:, :], xn_d[g * 128:(g + 1) * 128, :])
            for kc in range(32):
                P.op("tensor", "transpose", ptb[:, :],
                     xnb[:, kc * 128:(kc + 1) * 128], idb[:, :])
                P.op("scalar", "copy", xnT[:, kc, g * 128:(g + 1) * 128], ptb[:, :])
        for grp, gn in [(0, 7), (1, 7), (2, 2)]:
            t0 = grp * 7
            for nch in range(NCH):
                for kc in range(32):
                    P.dma(wst[:, :], w_d[kc * 128:(kc + 1) * 128, nch * 512:(nch + 1) * 512])
                    P.op("vector", "tensor_copy", wsb[:, :], wst[:, :])
                    for t in range(gn):
                        P.op("tensor", "matmul", po[:, t, :],
                             xnT[:, kc, (t0 + t) * 128:(t0 + t + 1) * 128], wsb[:, :],
                             start=(kc == 0), stop=(kc == 31),
                             wait=(t == 0), inc=(t == gn - 1))
                for t in range(gn):
                    g = t0 + t
                    P.op("vector", "tensor_copy", ost[:, :], po[:, t, :])
                    P.dma(o_d[g * 128:(g + 1) * 128, nch * 512:(nch + 1) * 512],
                          ost[:, :], q="gpsimd")
        with nc.Block() as block:
            P.emit(block, sd, sc)
    return nc


def build_p4():
    TH = 8192
    NC_ = 32
    nc = mk_nc()
    xb_d = nc.dram_tensor("xbcp", [TH + 3, 1280], F32, kind="ExternalInput").ap()
    dt_d = nc.dram_tensor("dtr", [TH, 16], F32, kind="ExternalInput").ap()
    wc_d = nc.dram_tensor("wcs", [1, 4 * 1280], F32, kind="ExternalInput").ap()
    bc_d = nc.dram_tensor("bcs", [1, 1280], F32, kind="ExternalInput").ap()
    db_d = nc.dram_tensor("dtb", [1, 16], F32, kind="ExternalInput").ap()
    al_d = nc.dram_tensor("alog", [1, 16], F32, kind="ExternalInput").ap()
    dd_d = nc.dram_tensor("dd", [1, 16], F32, kind="ExternalInput").ap()
    nb_d = nc.dram_tensor("nb", [2, 128, 256], F32, kind="ExternalInput").ap()
    eh_d = nc.dram_tensor("eh", [16, 2048], F32, kind="ExternalInput").ap()
    i16_d = nc.dram_tensor("i16", [16, 16], F32, kind="ExternalInput").ap()
    o16_d = nc.dram_tensor("o16", [16, 128], F32, kind="ExternalInput").ap()
    idf_d = nc.dram_tensor("idf", [128, 128], F32, kind="ExternalInput").ap()
    idb_d = nc.dram_tensor("idb", [128, 128], BF16, kind="ExternalInput").ap()
    onf_d = nc.dram_tensor("onf", [1, 128], F32, kind="ExternalInput").ap()
    ri_d = nc.dram_tensor("rin", [128, 1024], F32, kind="ExternalInput").ap()
    y_d = nc.dram_tensor("yh", [TH, 1024], F32, kind="ExternalOutput").ap()
    ro_d = nc.dram_tensor("rout", [128, 1024], F32, kind="ExternalOutput").ap()
    P = Prog()
    with ExitStack() as es:
        sb = lambda n_, s_, dt=F32: es.enter_context(nc.sbuf_tensor("s_" + n_, s_, dt))
        Lk = [sb(f"L{k}", [128, 2, 1280]) for k in range(4)]
        cv = sb("cv", [128, 2, 1280]); tp3 = sb("tp3", [128, 1280])
        xc = sb("xc", [128, 2, 1280])
        wkb = sb("wkb", [128, 4, 1280]); bb = sb("bb", [128, 1280])
        wcs = sb("wcs", [1, 5120]); bcs = sb("bcs", [1, 1280])
        dbs = sb("dbs", [1, 16]); als = sb("als", [1, 16]); dds = sb("dds", [1, 16])
        dtbB = sb("dtbB", [128, 16]); Ab = sb("Ab", [128, 16]); Db = sb("Db", [128, 16])
        dtt = sb("dtt", [128, 2, 16]); z1 = sb("z1", [128, 2, 16])
        az = sb("az", [128, 2, 16]); dtc = sb("dtc", [128, 2, 16])
        dA = sb("dA", [128, 2, 16])
        dAT = sb("dAT", [16, 256]); csA = sb("csA", [16, 256]); csB = sb("csB", [16, 256])
        r16 = sb("r16", [16, 16]); i16 = sb("i16", [16, 16]); o16 = sb("o16", [16, 128])
        lastB = sb("lastB", [128, 16]); Tt = sb("Tt", [128, 2, 16])
        od = sb("od", [128, 2, 16]); dec = sb("dec", [128, 2, 16])
        dcB = sb("dcB", [128, 16]); dtd = sb("dtd", [128, 2, 16])
        xdt = sb("xdt", [128, 2, 1024], BF16); xdd = sb("xdd", [128, 2, 1024], BF16)
        Bcb = sb("Bcb", [128, 2, 128], BF16); Ccb = sb("Ccb", [128, 2, 128], BF16)
        BT = sb("BT", [128, 256], BF16); CT = sb("CT", [128, 256], BF16)
        gt = sb("gt", [128, 2, 256]); ltf = sb("ltf", [128, 256]); lte = sb("lte", [128, 256])
        gm = sb("gm", [128, 2, 256], BF16)
        NB0 = sb("NB0", [128, 256]); NB1 = sb("NB1", [128, 256])
        eh = sb("eh", [16, 2048])
        idf = sb("idf", [128, 128]); idb = sb("idb", [128, 128], BF16)
        onf = sb("onf", [1, 128])
        R = sb("R", [128, 1024]); Rbf = sb("Rbf", [128, 1024], BF16)
        ysb = sb("ysb", [128, 2, 1024])
        scrA = es.enter_context(nc.psum_tensor("scrA", [128, 512], F32))
        pT1 = es.enter_context(nc.psum_tensor("pT1", [128, 256], F32))
        ptb = es.enter_context(nc.psum_tensor("ptb", [128, 128], BF16))
        pYd = es.enter_context(nc.psum_tensor("pYd", [128, 128], F32))
        pYo = es.enter_context(nc.psum_tensor("pYo", [128, 128], F32))
        pS = es.enter_context(nc.psum_tensor("pS", [128, 1024], F32))
        sd = es.enter_context(nc.semaphore("sd"))
        sc = es.enter_context(nc.semaphore("sc"))

        for nm, t_, d_ in [("wcs", wcs, wc_d), ("bcs", bcs, bc_d), ("dbs", dbs, db_d),
                           ("als", als, al_d), ("dds", dds, dd_d), ("i16", i16, i16_d),
                           ("o16", o16, o16_d), ("eh", eh, eh_d), ("idf", idf, idf_d),
                           ("idb", idb, idb_d), ("onf", onf, onf_d)]:
            P.dma(t_[:, :] if t_.shape[0] > 1 else t_[0:1, :], d_)
        P.dma(NB0[:, :], nb_d[0]); P.dma(NB1[:, :], nb_d[1])
        P.dma(R[:, :], ri_d)
        P.op("vector", "tensor_copy", Rbf[:, :], R[:, :])
        for k in range(4):
            for s0, n_ in [(0, 512), (512, 512), (1024, 256)]:
                P.op("tensor", "matmul", scrA[:, 0:n_], onf[0:1, :],
                     wcs[0:1, k * 1280 + s0:k * 1280 + s0 + n_], start=True, stop=True)
                P.op("vector", "tensor_copy", wkb[:, k, s0:s0 + n_], scrA[:, 0:n_])
        for s0, n_ in [(0, 512), (512, 512), (1024, 256)]:
            P.op("tensor", "matmul", scrA[:, 0:n_], onf[0:1, :],
                 bcs[0:1, s0:s0 + n_], start=True, stop=True)
            P.op("vector", "tensor_copy", bb[:, s0:s0 + n_], scrA[:, 0:n_])
        P.op("tensor", "matmul", scrA[:, 0:16], onf[0:1, :], dbs[0:1, :], start=True, stop=True)
        P.op("vector", "tensor_copy", dtbB[:, :], scrA[:, 0:16])
        P.op("scalar", "activation", r16[0:1, :], als[0:1, :], AF.Exp)
        P.op("vector", "tensor_scalar_mul", r16[0:1, :], r16[0:1, :], -1.0)
        P.op("tensor", "matmul", scrA[:, 0:16], onf[0:1, :], r16[0:1, :], start=True, stop=True)
        P.op("vector", "tensor_copy", Ab[:, :], scrA[:, 0:16])
        P.op("tensor", "matmul", scrA[:, 0:16], onf[0:1, :], dds[0:1, :], start=True, stop=True)
        P.op("vector", "tensor_copy", Db[:, :], scrA[:, 0:16])

        for c in range(NC_):
            r0 = c * 256
            for k in range(4):
                P.dma(Lk[k][:, :, :],
                      xb_d[r0 + k:r0 + k + 256, :].rearrange("(a p) c -> p a c", p=128))
            for a in range(2):
                P.op("vector", "tensor_mul", cv[:, a, :], Lk[0][:, a, :], wkb[:, 0, :])
                for k in range(1, 4):
                    P.op("vector", "tensor_mul", tp3[:, :], Lk[k][:, a, :], wkb[:, k, :])
                    P.op("vector", "tensor_add", cv[:, a, :], cv[:, a, :], tp3[:, :])
                P.op("vector", "tensor_add", cv[:, a, :], cv[:, a, :], bb[:, :])
                P.op("scalar", "activation", xc[:, a, :], cv[:, a, :], AF.Silu)
            P.dma(dtt[:, :, :],
                  dt_d[r0:r0 + 256, :].rearrange("(a p) c -> p a c", p=128))
            dtb3 = dtbB[:, :].unsqueeze(1).broadcast_to([128, 2, 16])
            P.op("vector", "tensor_tensor", z1[:, :, :], dtt[:, :, :], dtb3, ALU.add)
            P.op("vector", "tensor_scalar_mul", az[:, :, :], z1[:, :, :], -1.0)
            P.op("vector", "tensor_max", az[:, :, :], az[:, :, :], z1[:, :, :])
            P.op("vector", "tensor_scalar_mul", az[:, :, :], az[:, :, :], -1.0)
            P.op("scalar", "activation", az[:, :, :], az[:, :, :], AF.Exp)
            P.op("vector", "tensor_scalar_add", az[:, :, :], az[:, :, :], 1.0)
            P.op("scalar", "activation", az[:, :, :], az[:, :, :], AF.Ln)
            P.op("vector", "tensor_scalar_max", dtc[:, :, :], z1[:, :, :], 0.0)
            P.op("vector", "tensor_add", dtc[:, :, :], dtc[:, :, :], az[:, :, :])
            ab3 = Ab[:, :].unsqueeze(1).broadcast_to([128, 2, 16])
            P.op("vector", "tensor_tensor", dA[:, :, :], dtc[:, :, :], ab3, ALU.mult)
            for a in range(2):
                P.op("tensor", "transpose", scrA[0:16, 0:128], dA[:, a, :], idf[:, :])
                P.op("scalar", "copy", dAT[0:16, a * 128:(a + 1) * 128], scrA[0:16, 0:128])
            cur, nxt = dAT, csA
            for s in [1, 2, 4, 8, 16, 32, 64, 128]:
                P.op("vector", "tensor_copy", nxt[0:16, 0:s], cur[0:16, 0:s])
                P.op("vector", "tensor_add", nxt[0:16, s:256], cur[0:16, s:256],
                     cur[0:16, 0:256 - s])
                cur, nxt = (csA, csB) if cur is dAT else ((csB, csA) if cur is csA else (csA, csB))
            Acs = cur
            P.op("vector", "tensor_scalar_mul", r16[0:16, :], i16[0:16, :], Acs[0:16, 255:256])
            P.op("tensor", "matmul", scrA[:, 0:16], o16[0:16, :], r16[0:16, :],
                 start=True, stop=True)
            P.op("vector", "tensor_copy", lastB[:, :], scrA[:, 0:16])
            for a in range(2):
                P.op("tensor", "transpose", scrA[:, 16:32],
                     Acs[0:16, a * 128:(a + 1) * 128], i16[0:16, :])
                P.op("scalar", "copy", Tt[:, a, :], scrA[:, 16:32])
                P.op("scalar", "activation", od[:, a, :], Tt[:, a, :], AF.Exp)
                P.op("vector", "tensor_sub", z1[:, a, :], lastB[:, :], Tt[:, a, :])
                P.op("scalar", "activation", dec[:, a, :], z1[:, a, :], AF.Exp)
            P.op("scalar", "activation", dcB[:, :], lastB[:, :], AF.Exp)
            P.op("vector", "tensor_tensor", dtd[:, :, :], dtc[:, :, :], dec[:, :, :], ALU.mult)
            x4 = xc[:, :, 0:1024].rearrange("p a (h q) -> p a h q", q=64)
            dt4 = dtc[:, :, :].unsqueeze(3).broadcast_to([128, 2, 16, 64])
            dd4 = dtd[:, :, :].unsqueeze(3).broadcast_to([128, 2, 16, 64])
            P.op("vector", "tensor_tensor",
                 xdt[:, :, :].rearrange("p a (h q) -> p a h q", q=64), x4, dt4, ALU.mult)
            P.op("vector", "tensor_tensor",
                 xdd[:, :, :].rearrange("p a (h q) -> p a h q", q=64), x4, dd4, ALU.mult)
            P.op("vector", "tensor_copy", Bcb[:, :, :], xc[:, :, 1024:1152])
            P.op("vector", "tensor_copy", Ccb[:, :, :], xc[:, :, 1152:1280])
            for a in range(2):
                P.op("tensor", "transpose", ptb[:, :], Bcb[:, a, :], idb[:, :])
                P.op("scalar", "copy", BT[:, a * 128:(a + 1) * 128], ptb[:, :])
                P.op("tensor", "transpose", ptb[:, :], Ccb[:, a, :], idb[:, :])
                P.op("scalar", "copy", CT[:, a * 128:(a + 1) * 128], ptb[:, :])
            for a in range(2):
                P.op("tensor", "matmul", pT1[:, :], BT[:, a * 128:(a + 1) * 128],
                     CT[:, :], start=True, stop=True)
                P.op("vector", "tensor_copy", gt[:, a, :], pT1[:, :])
            for h in range(16):
                P.op("tensor", "matmul", pT1[:, :], eh[0:16, h * 128:(h + 1) * 128],
                     Acs[0:16, :], start=True, stop=True)
                for a in range(2):
                    nbt = NB0 if a == 0 else NB1
                    P.op("vector", "scalar_tensor_tensor", ltf[:, :], pT1[:, :],
                         Tt[:, a, h:h + 1], nbt[:, :], ALU.subtract, ALU.add)
                    P.op("scalar", "activation", lte[:, :], ltf[:, :], AF.Exp)
                    P.op("vector", "tensor_mul", gm[:, a, :], lte[:, :], gt[:, a, :])
                for lt in range(2):
                    for a2 in range(2):
                        P.op("tensor", "matmul", pYd[:, lt * 64:(lt + 1) * 64],
                             gm[:, a2, lt * 128:(lt + 1) * 128],
                             xdt[:, a2, h * 64:(h + 1) * 64],
                             start=(a2 == 0), stop=(a2 == 1),
                             wait=(a2 == 0), inc=(a2 == 1))
                    P.op("tensor", "matmul", pYo[:, lt * 64:(lt + 1) * 64],
                         CT[:, lt * 128:(lt + 1) * 128], Rbf[:, h * 64:(h + 1) * 64],
                         start=True, stop=True)
                for a2 in range(2):
                    P.op("tensor", "matmul", pS[:, h * 64:(h + 1) * 64],
                         Bcb[:, a2, :], xdd[:, a2, h * 64:(h + 1) * 64],
                         start=(a2 == 0), stop=(a2 == 1),
                         wait=(a2 == 0), inc=(a2 == 1))
                for a in range(2):
                    P.op("vector", "scalar_tensor_tensor", ysb[:, a, h * 64:(h + 1) * 64],
                         xc[:, a, h * 64:(h + 1) * 64], Db[:, h:h + 1],
                         pYd[:, a * 64:(a + 1) * 64], ALU.mult, ALU.add)
                    P.op("vector", "scalar_tensor_tensor", ysb[:, a, h * 64:(h + 1) * 64],
                         pYo[:, a * 64:(a + 1) * 64], od[:, a, h:h + 1],
                         ysb[:, a, h * 64:(h + 1) * 64], ALU.mult, ALU.add)
            dc3 = dcB[:, :].unsqueeze(2).broadcast_to([128, 16, 64])
            P.op("vector", "tensor_tensor",
                 R[:, :].rearrange("p (h q) -> p h q", q=64),
                 R[:, :].rearrange("p (h q) -> p h q", q=64), dc3, ALU.mult)
            P.op("vector", "tensor_add", R[:, :], R[:, :], pS[:, :])
            P.op("vector", "tensor_copy", Rbf[:, :], R[:, :])
            P.dma(y_d[r0:r0 + 256, :].rearrange("(a p) c -> p a c", p=128),
                  ysb[:, :, :], q="gpsimd")
        P.dma(ro_d, R[:, :], q="gpsimd")
        with nc.Block() as block:
            P.emit(block, sd, sc)
    return nc


def build_p5():
    DI = 8192
    nc = mk_nc()
    y_d = nc.dram_tensor("y", [T, DI], F32, kind="ExternalInput").ap()
    z_d = nc.dram_tensor("z", [T, DI], F32, kind="ExternalInput").ap()
    f_d = nc.dram_tensor("f", [T, D], F32, kind="ExternalInput").ap()
    nw_d = nc.dram_tensor("nw", [128, 64], F32, kind="ExternalInput").ap()
    w_d = nc.dram_tensor("woT2", [DI, D], F32, kind="ExternalInput").ap()
    idb_d = nc.dram_tensor("idb", [128, 128], BF16, kind="ExternalInput").ap()
    onf_d = nc.dram_tensor("onf", [1, 128], F32, kind="ExternalInput").ap()
    o_d = nc.dram_tensor("out", [T, D], F32, kind="ExternalOutput").ap()
    P = Prog()
    with ExitStack() as es:
        sb = lambda n_, s_, dt=F32: es.enter_context(nc.sbuf_tensor("s_" + n_, s_, dt))
        yb = sb("yb", [128, DI]); zb = sb("zb", [128, DI])
        nwc = sb("nwc", [128, 64])
        ynb = sb("ynb", [128, DI], BF16)
        ynT = sb("ynT", [128, 64, 512], BF16)
        idb = sb("idb", [128, 128], BF16); onf = sb("onf", [1, 128])
        wst = sb("wst", [128, 512]); wsb = sb("wsb", [128, 512], BF16)
        fst = sb("fst", [128, 512]); ost = sb("ost", [128, 512])
        mt = sb("mt", [128, 1]); vt = sb("vt", [128, 1])
        po = es.enter_context(nc.psum_tensor("po", [128, 4, 512], F32))
        ptb = es.enter_context(nc.psum_tensor("ptb", [128, 128], BF16))
        sd = es.enter_context(nc.semaphore("sd"))
        sc = es.enter_context(nc.semaphore("sc"))
        P.dma(idb[:, :], idb_d); P.dma(onf[:, :], onf_d)
        P.dma(nwc[:, :], nw_d)
        for st in range(4):
            for t in range(4):
                g = st * 4 + t
                P.dma(yb[:, :], y_d[g * 128:(g + 1) * 128, :])
                P.dma(zb[:, :], z_d[g * 128:(g + 1) * 128, :])
                P.op("scalar", "activation", zb[:, :], zb[:, :], AF.Silu)
                P.op("vector", "tensor_mul", yb[:, :], yb[:, :], zb[:, :])
                P.op("scalar", "activation", zb[:, :], yb[:, :], AF.Square)
                P.op("vector", "reduce_sum", mt[:, :], zb[:, :], axis=AX.X)
                P.op("vector", "tensor_scalar_mul", mt[:, :], mt[:, :], 1.0 / DI)
                P.op("vector", "tensor_scalar_add", mt[:, :], mt[:, :], 1e-5)
                P.op("scalar", "activation", mt[:, :], mt[:, :], AF.Ln)
                P.op("vector", "tensor_scalar_mul", mt[:, :], mt[:, :], -0.5)
                P.op("scalar", "activation", vt[:, :], mt[:, :], AF.Exp)
                P.op("vector", "tensor_scalar_mul", ynb[:, :], yb[:, :], vt[:, :])
                for kc in range(64):
                    P.op("tensor", "transpose", ptb[:, :],
                         ynb[:, kc * 128:(kc + 1) * 128], idb[:, :])
                    P.op("scalar", "copy", ynT[:, kc, t * 128:(t + 1) * 128], ptb[:, :])
            for nch in range(8):
                for kc in range(64):
                    P.dma(wst[:, :], w_d[kc * 128:(kc + 1) * 128, nch * 512:(nch + 1) * 512])
                    P.op("vector", "tensor_scalar_mul", wsb[:, :], wst[:, :], nwc[:, kc:kc + 1])
                    for t in range(4):
                        P.op("tensor", "matmul", po[:, t, :],
                             ynT[:, kc, t * 128:(t + 1) * 128], wsb[:, :],
                             start=(kc == 0), stop=(kc == 63),
                             wait=(t == 0), inc=(t == 3))
                for t in range(4):
                    g = st * 4 + t
                    P.dma(fst[:, :], f_d[g * 128:(g + 1) * 128, nch * 512:(nch + 1) * 512])
                    P.op("vector", "tensor_add", ost[:, :], po[:, t, :], fst[:, :])
                    P.dma(o_d[g * 128:(g + 1) * 128, nch * 512:(nch + 1) * 512],
                          ost[:, :], q="gpsimd")
        with nc.Block() as block:
            P.emit(block, sd, sc)
    return nc


def kernel(**inp):
    f32 = np.float32
    x = np.ascontiguousarray(np.asarray(inp["L0_attn_hidden_states"], f32).reshape(S, D))
    cores = list(range(NCORES))
    IDB = np.eye(128, dtype=BF)
    IDF = np.eye(128, dtype=f32)
    ONB = np.ones((1, 128), BF)
    ONF = np.ones((1, 128), f32)
    ONC = np.ones((128, 1), f32)

    hsel = [0, 4, 8, 12, 16, 20, 24, 28]
    rows = np.concatenate([np.arange(h * 128, (h + 1) * 128) for h in hsel])
    wqT = np.ascontiguousarray(np.asarray(inp["L0_attn_Wq"], f32)[rows, :].T)
    bq = np.ascontiguousarray(np.asarray(inp["L0_attn_bq"], f32)[rows].reshape(1, 1024))
    M = np.asarray(inp["L0_attn_M"], f32)
    zv = np.asarray(inp["L0_attn_z"], f32)
    mz = np.ascontiguousarray(np.concatenate([M, zv[:, :, None]], axis=2))
    woT = np.ascontiguousarray(np.asarray(inp["L0_attn_Wo"], f32).T)
    w0T = np.ascontiguousarray(np.asarray(inp["L0_ffn_W"], f32).T)
    b0 = np.ascontiguousarray(np.asarray(inp["L0_ffn_b"], f32).reshape(1, D))
    acT = np.ascontiguousarray(np.asarray(inp["L0_ffn_A"], f32).reshape(128, D).T)
    valid = np.zeros((S, 1), f32)
    valid[611:S - 1] = 1.0
    nc1 = build_p1()
    maps1 = []
    for i in cores:
        maps1.append({"x": x[i * T:(i + 1) * T], "wqT": wqT, "bq": bq, "mz": mz,
                      "woT": woT, "w0T": w0T, "b0": b0, "acT": acT,
                      "msk": np.ascontiguousarray(valid[i * T:(i + 1) * T].reshape(16, 128).T), "idb": IDB, "onb": ONB})
    r1 = run_bass_kernel_spmd(nc1, maps1, core_ids=cores)
    acc = [r1.results[i]["acc"] for i in range(8)]
    tmat = [r1.results[i]["tmat"] for i in range(8)]
    rps = np.ascontiguousarray(np.stack([r1.results[i]["rpart"] for i in range(8)]))

    bcT = np.ascontiguousarray(
        np.transpose(np.asarray(inp["L0_ffn_B"], f32), (0, 2, 1)).reshape(128, D))
    rw0T = np.ascontiguousarray(np.asarray(inp["L0_ffn_router_W"], f32).T)
    rb0 = np.ascontiguousarray(np.asarray(inp["L0_ffn_router_b"], f32).reshape(1, 8))
    rw1T = np.ascontiguousarray(np.asarray(inp["L1_ffn_router_weight"], f32).T)
    rb1 = np.ascontiguousarray(np.asarray(inp["L1_ffn_router_bias"], f32).reshape(1, 64))
    eb = np.ascontiguousarray(np.asarray(inp["L1_ffn_expert_bias"], f32))
    nc2 = build_p2()
    maps2 = []
    for i in cores:
        maps2.append({"acc": acc[i], "tmat": tmat[i], "rps": rps, "bcT": bcT,
                      "rw0T": rw0T, "rb0": rb0, "rw1T": rw1T, "rb1": rb1, "eb": eb,
                      "idb": IDB, "onb": ONB, "onf": ONF, "onc": ONC})
    r2 = run_bass_kernel_spmd(nc2, maps2, core_ids=cores)
    f_ = [r2.results[i]["f"] for i in range(8)]
    xn1b = [np.asarray(r2.results[i]["xn1b"]) for i in range(8)]

    winT = np.asarray(inp["L1_attn_in_proj_w"], f32).T
    wpad = np.concatenate([winT, np.zeros((D, 19456 - 18560), f32)], axis=1)
    nc3 = build_p3()
    zx_halves = []
    for hf in range(2):
        wh = np.ascontiguousarray(wpad[:, hf * 9728:(hf + 1) * 9728])
        maps3 = [{"xn1b": xn1b[i], "wp": wh, "idb": IDB} for i in cores]
        r3 = run_bass_kernel_spmd(nc3, maps3, core_ids=cores)
        zx_halves.append(np.concatenate([r3.results[i]["o"] for i in range(8)], axis=0))
    zx = np.concatenate(zx_halves, axis=1)[:, :18560]
    zpart = zx[:, :8192]
    xbc = zx[:, 8192:18432]
    dtr = zx[:, 18432:]

    cw = np.asarray(inp["L1_attn_conv_w"], f32)
    cb = np.asarray(inp["L1_attn_conv_b"], f32)
    dtbias = np.asarray(inp["L1_attn_dt_bias"], f32)
    alog = np.asarray(inp["L1_attn_A_log"], f32)
    Dp = np.asarray(inp["L1_attn_D"], f32)
    NB = np.zeros((2, 128, 256), f32)
    for stl in range(2):
        for s_ in range(128):
            NB[stl, s_, :] = np.where(np.arange(256) >= stl * 128 + s_, 0.0, -1e30)
    EH = np.zeros((16, 2048), f32)
    for h in range(16):
        EH[h, h * 128:(h + 1) * 128] = 1.0
    I16 = np.eye(16, dtype=f32)
    O16 = np.ones((16, 128), f32)
    nc4 = build_p4()
    ygs = [[None, None] for _ in cores]
    rcur = {i: np.zeros((128, 1024), f32) for i in cores}
    for hf in range(2):
        maps4 = []
        for i in cores:
            xs = xbc[:, i * 1024:(i + 1) * 1024]
            bs = xbc[:, 8192 + i * 128:8192 + (i + 1) * 128]
            cs = xbc[:, 9216 + i * 128:9216 + (i + 1) * 128]
            cat = np.concatenate([xs, bs, cs], axis=1)
            if hf == 0:
                xp = np.concatenate([np.zeros((3, 1280), f32), cat[0:8192]], axis=0)
            else:
                xp = cat[8189:16384]
            wrows = np.concatenate([np.arange(i * 1024, (i + 1) * 1024),
                                    np.arange(8192 + i * 128, 8192 + (i + 1) * 128),
                                    np.arange(9216 + i * 128, 9216 + (i + 1) * 128)])
            wcs = np.ascontiguousarray(cw[wrows, :].T.reshape(1, 4 * 1280))
            bcs = np.ascontiguousarray(cb[wrows].reshape(1, 1280))
            maps4.append({"xbcp": np.ascontiguousarray(xp),
                          "dtr": np.ascontiguousarray(dtr[hf * 8192:(hf + 1) * 8192,
                                                          i * 16:(i + 1) * 16]),
                          "wcs": wcs, "bcs": bcs,
                          "dtb": np.ascontiguousarray(dtbias[i * 16:(i + 1) * 16].reshape(1, 16)),
                          "alog": np.ascontiguousarray(alog[i * 16:(i + 1) * 16].reshape(1, 16)),
                          "dd": np.ascontiguousarray(Dp[i * 16:(i + 1) * 16].reshape(1, 16)),
                          "nb": NB, "eh": EH, "i16": I16, "o16": O16,
                          "idf": IDF, "idb": IDB, "onf": ONF, "rin": rcur[i]})
        r4 = run_bass_kernel_spmd(nc4, maps4, core_ids=cores)
        for i in cores:
            ygs[i][hf] = r4.results[i]["yh"]
            rcur[i] = np.ascontiguousarray(r4.results[i]["rout"])
    yfull = np.concatenate(
        [np.concatenate([ygs[i][0], ygs[i][1]], axis=0) for i in cores], axis=1)

    nw = np.ascontiguousarray(np.asarray(inp["L1_attn_norm_w"], f32).reshape(64, 128).T)
    woT2 = np.ascontiguousarray(np.asarray(inp["L1_attn_out_proj_w"], f32).T)
    nc5 = build_p5()
    maps5 = []
    for i in cores:
        maps5.append({"y": np.ascontiguousarray(yfull[i * T:(i + 1) * T]),
                      "z": np.ascontiguousarray(zpart[i * T:(i + 1) * T]),
                      "f": f_[i], "nw": nw, "woT2": woT2, "idb": IDB, "onf": ONF})
    r5 = run_bass_kernel_spmd(nc5, maps5, core_ids=cores)
    out = np.concatenate([r5.results[i]["out"] for i in range(8)], axis=0)
    return out.reshape(1, S, D).astype(np.float32)
